# version 05
# baseline (speedup 1.0000x reference)
import numpy as np
import ml_dtypes
from contextlib import ExitStack

import concourse.bass as bass
import concourse.mybir as mybir
from concourse.bass_utils import run_bass_kernel_spmd

F32 = mybir.dt.float32
BF16 = mybir.dt.bfloat16
AF = mybir.ActivationFunctionType
ALU = mybir.AluOpType
AXX = mybir.AxisListType.X

D = 4096
R = 2048
NH = 32
NCHUNK = 32
SCALE = float(1.0 / np.sqrt(128.0))
LN_EPS = 1e-5
NCORES = 8

STAGE = 6
DEBUG = ()



def _p1_tiles(lhsT):
    t = lhsT.reshape(32, 128, 32, 128)
    return np.ascontiguousarray(t.transpose(2, 1, 0, 3)).reshape(32, 128, 4096)


def _p2_blocks(rhsW):
    t = rhsW.reshape(32, 128, 16, 256)
    return np.ascontiguousarray(t.transpose(2, 1, 0, 3)).reshape(16, 128, 32 * 256)


def _rows_of_core(c):
    r = c % 2
    return np.concatenate([np.arange(512 * (2 * g + r), 512 * (2 * g + r) + 512)
                           for g in range(4)])


def _rope_tabs(tidx, style):
    if style == "a":
        inv = 1.0 / (10000.0 ** (np.arange(0, 128, 2, dtype=np.float64) / 128.0))
        th = tidx.astype(np.float64)[None, :] * inv[:, None]
    else:
        th = np.broadcast_to(tidx.astype(np.float64)[None, :], (64, len(tidx)))
    c, s = np.cos(th), np.sin(th)
    tabC = np.concatenate([c, c], 0).astype(np.float32)
    tabS = np.concatenate([s, -s], 0).astype(np.float32)
    return np.ascontiguousarray(tabC), np.ascontiguousarray(tabS)


def stage_inputs(ins):
    perm = np.asarray(ins["L0_ffn_perm"]).astype(np.int64)
    x = np.asarray(ins["L0_attn_a_hidden_states"])
    rp = np.concatenate([128 * h + np.concatenate([np.arange(0, 128, 2),
                                                   np.arange(1, 128, 2)])
                         for h in range(NH)])
    shared = {
        "wq": _p1_tiles(np.asarray(ins["L0_attn_a_q_w"])[:, perm].T),
        "wk": _p1_tiles(np.asarray(ins["L0_attn_a_k_w"])[:, perm].T),
        "wv": _p2_blocks(np.asarray(ins["L0_attn_a_v_w"])[:, perm].T),
        "wo": _p1_tiles(np.asarray(ins["L0_attn_a_o_w"])[perm, :].T),
        "wqb": _p1_tiles(np.asarray(ins["L0_attn_b_wq"])[rp][:, perm].T),
        "wkb": _p1_tiles(np.asarray(ins["L0_attn_b_wk"])[rp][:, perm].T),
        "wvb": _p2_blocks(np.asarray(ins["L0_attn_b_wv"])[:, perm].T),
        "wob": _p1_tiles(np.asarray(ins["L0_attn_b_wo"])[perm, :].T),
        "upt": np.ascontiguousarray(
            np.asarray(ins["L0_ffn_up"]).reshape(16, 2, 128, 8, 128)
            .transpose(0, 2, 3, 1, 4)).reshape(16, 128, 2048),
        "dnt": np.ascontiguousarray(
            np.asarray(ins["L0_ffn_down"]).reshape(16, 8, 128, 2, 128)
            .transpose(0, 2, 1, 3, 4)).reshape(16, 128, 2048),
        "keypos": np.ascontiguousarray(
            (128.0 * np.arange(32)[None, :] +
             np.arange(128)[:, None]).astype(np.float32)),
        "ident": np.eye(128, dtype=ml_dtypes.bfloat16),
    }
    in_maps = []
    for c in range(NCORES):
        b = c // 2
        tidx = _rows_of_core(c)
        m = dict(shared)
        m["x0"] = np.ascontiguousarray(x[b][tidx][:, perm].T)
        m["tca"], m["tsa"] = _rope_tabs(tidx, "a")
        m["tcb"], m["tsb"] = _rope_tabs(tidx, "b")
        m["qposb"] = np.ascontiguousarray(
            np.broadcast_to(tidx.astype(np.float32)[None, :], (128, R)))
        wt = ((tidx >= 513) & (tidx < 4040)).astype(np.float32) \
            if c < 2 else np.zeros(R, np.float32)
        m["wtb"] = np.ascontiguousarray(np.broadcast_to(wt[None, :], (128, R)))
        in_maps.append(m)
    return in_maps, perm



class K:
    ENGS = ("sync", "tensor", "vector", "scalar", "gpsimd")

    def __init__(self, nc, es):
        self.nc, self.es = nc, es
        self.ops = {e: [] for e in self.ENGS}
        self.cnt, self.sem = {}, {}
        self.scopes = []

    def mksem(self, *names):
        for n in names:
            self.sem[n] = self.es.enter_context(self.nc.semaphore(n))
            self.cnt[n] = 0

    def inc(self, name, n=1):
        self.cnt[name] += n
        return self.cnt[name]

    def cur(self, name):
        return self.cnt[name]

    def on(self, e, fn):
        self.ops[e].append(fn)

    def sb(self, name, shape, dtype):
        return self.es.enter_context(self.nc.sbuf_tensor(name, shape, dtype))

    def open_scope(self):
        self.scopes.append(ExitStack())

    def close_scope(self):
        self.scopes.pop().close()

    def sbl(self, name, shape, dtype):
        return self.scopes[-1].enter_context(
            self.nc.sbuf_tensor(name, shape, dtype))

    def psl(self, name, shape, dtype=F32):
        return self.scopes[-1].enter_context(
            self.nc.psum_tensor(name, shape, dtype))

    def gate(self):
        return self.cur("dv"), self.cur("ac")


def build_program(stage=None, debug=None, detect_races=True):
    stage = STAGE if stage is None else stage
    debug = DEBUG if debug is None else debug
    nc = bass.Bass("TRN2", target_bir_lowering=False,
                   detect_race_conditions=detect_races)
    dt = nc.dram_tensor
    dram = {}
    dram["x0"] = dt("x0", [D, R], F32, kind="ExternalInput")
    for n in ("wq", "wk", "wo", "wqb", "wkb", "wob"):
        dram[n] = dt(n, [32, 128, 4096], F32, kind="ExternalInput")
    for n in ("wv", "wvb"):
        dram[n] = dt(n, [16, 128, 32 * 256], F32, kind="ExternalInput")
    dram["upt"] = dt("upt", [16, 128, 2048], F32, kind="ExternalInput")
    dram["dnt"] = dt("dnt", [16, 128, 2048], F32, kind="ExternalInput")
    for n in ("tca", "tsa", "tcb", "tsb", "qposb", "wtb"):
        dram[n] = dt(n, [128, R], F32, kind="ExternalInput")
    dram["keypos"] = dt("keypos", [128, 32], F32, kind="ExternalInput")
    dram["ident"] = dt("ident", [128, 128], BF16, kind="ExternalInput")
    dram["y"] = dt("y", [D, R], F32, kind="ExternalOutput")

    def internal(name, shape, dtyp):
        dram[name] = dt(name, shape, dtyp, kind="Internal")
        if name in debug:
            dram["dbg_" + name] = dt("dbg_" + name, shape, dtyp,
                                     kind="ExternalOutput")

    internal("xres", [D, R], F32)
    internal("q_scr", [NH, 128, R], BF16)
    internal("k_cci", [8, 512, R], BF16)
    internal("v_cci", [8, R, 512], BF16)
    internal("k_cco", [8, 1024, R], BF16)
    internal("v_cco", [8, 2 * R, 512], BF16)
    internal("stat_d", [2, R], F32)
    internal("imp_d", [1, 512], F32)
    internal("imp_ar", [1, 512], F32)
    internal("pad_d", [1, 640], F32)
    internal("xn_d", [128, NCHUNK * R], BF16)

    es = ExitStack()
    k = K(nc, es)
    k.dram = dram
    k.stage = stage
    k.mksem("sx", "ssc", "sg", "pe", "dv", "ac", "im0")
    for st_ in ("ak", "av", "bk", "bv"):
        for i in range(8):
            k.mksem(f"{st_}{i}")

    k.XN = k.sb("XN", [128, NCHUNK * R], BF16)
    k.kpos = k.sb("kpos", [128, 32], F32)
    k.ones_c = k.sb("ones_c", [128, 1], BF16)
    k.ones_r = k.sb("ones_r", [1, 128], BF16)
    k.qsum = k.sb("qsum", [128, 32], F32)
    k.qsum_bf = k.sb("qsum_bf", [128, 32], BF16)
    k.kmask = k.sb("kmask", [128, 8], F32)
    k.vbias = k.sb("vbias", [128, 8], F32)
    k.ident = k.sb("ident_sb", [128, 128], BF16)

    emit_preload(k)
    emit_attn_phaseA(k, 0, "a")
    if stage >= 2:
        emit_mask(k)
    if stage >= 3:
        emit_attention(k, 0, "a")
        emit_ow(k, 0, "a", dst="xres")
    if stage >= 4:
        emit_ffn(k, 1, src="xres", dst="xres", ready=k.res_sc)
    if stage >= 5:
        emit_attn_phaseA(k, 2, "b", ready=k.res_sc)
        emit_attention(k, 2, "b")
        emit_ow(k, 2, "b", dst="xres", ready=k.res_sc)
    if stage >= 6:
        emit_ffn(k, 3, src="xres", dst="y", ready=k.res_sc)
    if debug:
        gates = {n: k.cur(n) for n in list(k.cnt)}
        dbg_names = [n for n in debug if n != "xn_d"]
        if "xn_d" in debug:
            k.inc("sg", 16)
        for n in dbg_names:
            k.inc("sg", 16)

        def f_dbg(e, gates=gates, dbg_names=dbg_names, debug=debug):
            for n, v in gates.items():
                if v > 0 and n != "pe":
                    e.wait_ge(k.sem[n], v)
            if "xn_d" in debug:
                e.dma_start(out=k.dram["dbg_xn_d"].ap(), in_=k.XN[:]
                            ).then_inc(k.sem["sg"], 16)
            for n in dbg_names:
                e.dma_start(out=k.dram["dbg_" + n].ap(), in_=k.dram[n].ap()
                            ).then_inc(k.sem["sg"], 16)
        k.on("gpsimd", f_dbg)
    emit_tail(k)

    prog = k.ops
    with es, nc.Block() as block:
        @block.sync
        def _(e):
            for f in prog["sync"]:
                f(e)

        @block.tensor
        def _(e):
            for f in prog["tensor"]:
                f(e)

        @block.vector
        def _(e):
            for f in prog["vector"]:
                f(e)

        @block.scalar
        def _(e):
            for f in prog["scalar"]:
                f(e)

        @block.gpsimd
        def _(e):
            for f in prog["gpsimd"]:
                f(e)
    return nc, k



def emit_preload(k):
    k.inc("sx", 16)

    k.inc("sx", 16)

    def f_sync(e):
        e.dma_start(out=k.kpos[:], in_=k.dram["keypos"].ap()
                    ).then_inc(k.sem["sx"], 16)
        e.dma_start(out=k.ident[:], in_=k.dram["ident"].ap()
                    ).then_inc(k.sem["sx"], 16)
    k.on("sync", f_sync)

    dvv = k.inc("dv", 2)

    def f_vec(e):
        e.memset(k.ones_c[:], 1.0).then_inc(k.sem["dv"], 1)
        e.memset(k.ones_r[:], 1.0).then_inc(k.sem["dv"], 1)
    k.on("vector", f_vec)
    k.ones_dv = dvv


def emit_ln(k, sub, src, ready=None):
    k.open_scope()
    gate_dv, gate_ac = k.gate()
    sum_ps = k.psl(f"sum_ps{sub}", [1, R], F32)
    sq_ps = k.psl(f"sq_ps{sub}", [1, R], F32)
    xf = [k.sbl(f"xf{sub}_{i}", [128, R], F32) for i in range(2)]
    xb = [k.sbl(f"xb{sub}_{i}", [128, R], BF16) for i in range(2)]
    xq = [k.sbl(f"xq{sub}_{i}", [128, R], BF16) for i in range(2)]
    mb = k.sbl(f"mb{sub}", [128, R], F32)
    rb = k.sbl(f"rb{sub}", [128, R], F32)
    srow = k.sbl(f"srow{sub}", [1, R], F32)
    qrow = k.sbl(f"qrow{sub}", [1, R], F32)

    ones_dv = k.ones_dv
    sx1, dv1, ac1, pe1 = [], [], [], []
    for ci in range(NCHUNK):
        sx1.append(k.inc("sx", 16))
        dv1.append(k.inc("dv"))
        ac1.append(k.inc("ac"))
        pe1.append(k.inc("pe", 8))

    def f_sync(e, src=src, ready=ready):
        e.wait_ge(k.sem["dv"], gate_dv)
        e.wait_ge(k.sem["ac"], gate_ac)
        for ci in range(NCHUNK):
            if ready is not None:
                e.wait_ge(k.sem[ready[0]], ready[1][ci])
            if ci >= 2:
                e.wait_ge(k.sem["dv"], dv1[ci - 2])
            e.dma_start(out=xf[ci % 2][:],
                        in_=k.dram[src].ap()[ci * 128:(ci + 1) * 128, :]
                        ).then_inc(k.sem["sx"], 16)
    k.on("sync", f_sync)

    def f_vec(e):
        for ci in range(NCHUNK):
            e.wait_ge(k.sem["sx"], sx1[ci])
            if ci >= 2:
                e.wait_ge(k.sem["ac"], ac1[ci - 2])
                e.wait_ge(k.sem["pe"], pe1[ci - 2])
            e.tensor_copy(xb[ci % 2][:], xf[ci % 2][:]).then_inc(k.sem["dv"], 1)
    k.on("vector", f_vec)

    def f_scal(e):
        e.wait_ge(k.sem["ac"], gate_ac)
        for ci in range(NCHUNK):
            e.wait_ge(k.sem["dv"], dv1[ci])
            if ci >= 2:
                e.wait_ge(k.sem["pe"], pe1[ci - 2])
            e.activation(xq[ci % 2][:], xb[ci % 2][:], AF.Square
                         ).then_inc(k.sem["ac"], 1)
    k.on("scalar", f_scal)

    def f_tens(e):
        e.wait_ge(k.sem["dv"], max(ones_dv, gate_dv))
        e.wait_ge(k.sem["ac"], gate_ac)
        for ci in range(NCHUNK):
            e.wait_ge(k.sem["dv"], dv1[ci])
            e.wait_ge(k.sem["ac"], ac1[ci])
            for cg in range(4):
                e.matmul(sum_ps[0:1, cg * 512:(cg + 1) * 512], k.ones_c[:],
                         xb[ci % 2][:, cg * 512:(cg + 1) * 512],
                         start=(ci == 0), stop=(ci == NCHUNK - 1)
                         ).then_inc(k.sem["pe"], 1)
            for cg in range(4):
                e.matmul(sq_ps[0:1, cg * 512:(cg + 1) * 512], k.ones_c[:],
                         xq[ci % 2][:, cg * 512:(cg + 1) * 512],
                         start=(ci == 0), stop=(ci == NCHUNK - 1)
                         ).then_inc(k.sem["pe"], 1)
    k.on("tensor", f_tens)

    pe_stats = k.cur("pe")
    dv_m = k.inc("dv")
    dv_q = k.inc("dv")
    dv_m2 = k.inc("dv")
    dv_var = k.inc("dv")

    def f_vec2(e):
        e.wait_ge(k.sem["pe"], pe_stats)
        e.tensor_single_scalar(srow[0:1, :], sum_ps[0:1, :], 1.0 / D, ALU.mult
                               ).then_inc(k.sem["dv"], 1)
        e.tensor_single_scalar(qrow[0:1, :], sq_ps[0:1, :], 1.0 / D, ALU.mult
                               ).then_inc(k.sem["dv"], 1)
        e.wait_ge(k.sem["dv"], dv_q)
        e.tensor_mul(mb[0:1, :], srow[0:1, :], srow[0:1, :]
                     ).then_inc(k.sem["dv"], 1)
        e.wait_ge(k.sem["dv"], dv_m2)
        e.tensor_sub(qrow[0:1, :], qrow[0:1, :], mb[0:1, :])
        e.tensor_single_scalar(qrow[0:1, :], qrow[0:1, :], LN_EPS, ALU.add
                               ).then_inc(k.sem["dv"], 1)
    k.on("vector", f_vec2)

    ac_sq = k.inc("ac")

    def f_scal2(e):
        e.wait_ge(k.sem["dv"], dv_var)
        e.activation(qrow[0:1, :], qrow[0:1, :], AF.Sqrt).then_inc(k.sem["ac"], 1)
    k.on("scalar", f_scal2)

    dv_rc = k.inc("dv")

    def f_vec3(e):
        e.wait_ge(k.sem["ac"], ac_sq)
        e.reciprocal(qrow[0:1, :], qrow[0:1, :]).then_inc(k.sem["dv"], 1)
    k.on("vector", f_vec3)

    sg_st = k.inc("sg", 32)
    sg_bc = k.inc("sg", 32)

    def f_g(e):
        e.wait_ge(k.sem["dv"], dv_rc)
        e.dma_start(out=k.dram["stat_d"].ap()[0:1, :], in_=srow[0:1, :]
                    ).then_inc(k.sem["sg"], 16)
        e.dma_start(out=k.dram["stat_d"].ap()[1:2, :], in_=qrow[0:1, :]
                    ).then_inc(k.sem["sg"], 16)
        e.wait_ge(k.sem["sg"], sg_st)
        e.dma_start(out=mb[:], in_=bass.AP(k.dram["stat_d"], 0, [[0, 128], [1, R]])
                    ).then_inc(k.sem["sg"], 16)
        e.dma_start(out=rb[:], in_=bass.AP(k.dram["stat_d"], R, [[0, 128], [1, R]])
                    ).then_inc(k.sem["sg"], 16)
    k.on("gpsimd", f_g)

    sx2, dva, dvb = [], [], []
    for ci in range(NCHUNK):
        sx2.append(k.inc("sx", 16))
        dva.append(k.inc("dv"))
        dvb.append(k.inc("dv"))

    def f_sync2(e, src=src, ready=ready):
        for ci in range(NCHUNK):
            if ready is not None:
                e.wait_ge(k.sem[ready[0]], ready[1][ci])
            if ci >= 2:
                e.wait_ge(k.sem["dv"], dvb[ci - 2])
            e.dma_start(out=xf[ci % 2][:],
                        in_=k.dram[src].ap()[ci * 128:(ci + 1) * 128, :]
                        ).then_inc(k.sem["sx"], 16)
    k.on("sync", f_sync2)

    def f_vec4(e):
        e.wait_ge(k.sem["sg"], sg_bc)
        for ci in range(NCHUNK):
            e.wait_ge(k.sem["sx"], sx2[ci])
            e.tensor_sub(xf[ci % 2][:], xf[ci % 2][:], mb[:]
                         ).then_inc(k.sem["dv"], 1)
            e.wait_ge(k.sem["dv"], dva[ci])
            e.tensor_mul(k.XN[:, ci * R:(ci + 1) * R], xf[ci % 2][:], rb[:]
                         ).then_inc(k.sem["dv"], 1)
    k.on("vector", f_vec4)
    k.ln_done_dv = dvb[-1]
    k.close_scope()


def emit_attn_phaseA(k, sub, style, ready=None):
    src = "x0" if sub == 0 else "xres"
    emit_ln(k, sub, src, ready)
    ln_dv = k.ln_done_dv

    k.open_scope()
    gate_dv, gate_ac = k.gate()
    pp = [k.psl(f"pp{sub}_{i}", [128, 512], F32) for i in range(8)]
    wband = [k.sbl(f"wband{sub}_{i}", [128, 4096], BF16) for i in range(2)]
    qtc = k.sbl(f"qtc{sub}", [128, R], F32)
    qts = k.sbl(f"qts{sub}", [128, R], F32)
    tabC = k.sbl(f"tabC{sub}", [128, R], F32)
    tabS = k.sbl(f"tabS{sub}", [128, R], F32)
    wtb = k.sbl(f"wtb{sub}", [128, R], F32) if style == "a" else None

    sx_tab = k.inc("sx", 32 + (16 if style == "a" else 0))

    def f_tab(e, style=style):
        e.wait_ge(k.sem["dv"], gate_dv)
        e.dma_start(out=tabC[:], in_=k.dram["tca" if style == "a" else "tcb"].ap()
                    ).then_inc(k.sem["sx"], 16)
        e.dma_start(out=tabS[:], in_=k.dram["tsa" if style == "a" else "tsb"].ap()
                    ).then_inc(k.sem["sx"], 16)
        if style == "a":
            e.dma_start(out=wtb[:], in_=k.dram["wtb"].ap()
                        ).then_inc(k.sem["sx"], 16)
    k.on("sync", f_tab)

    wq = "wq" if style == "a" else "wqb"
    wk = "wk" if style == "a" else "wkb"
    wv = "wv" if style == "a" else "wvb"

    for wname, which in ((wq, "q"), (wk, "k")):
        pe_co = [k.inc("pe", 4) for _ in range(NCHUNK)]
        sg_w = [k.inc("sg", 16), k.inc("sg", 16)]
        sg_sw, sg_st = [], []
        for co in range(NCHUNK):
            sg_sw.append(k.inc("sg", 32))
            sg_st.append(k.inc("sg", 16))
            if co + 2 < NCHUNK:
                sg_w.append(k.inc("sg", 16))
        dv_e, qs_dv = [], []
        for co in range(NCHUNK):
            k.inc("dv", 8)
            dv_e.append(k.cur("dv"))
            if style == "a" and which == "q":
                qs_dv.append(k.inc("dv"))

        def f_g(e, wname=wname, which=which, sg_w=list(sg_w), pe_co=pe_co,
                dv_e=dv_e, sg_sw=sg_sw):
            e.wait_ge(k.sem["dv"], gate_dv)
            e.dma_start(out=wband[0][:], in_=k.dram[wname].ap()[0]
                        ).then_inc(k.sem["sg"], 16)
            e.dma_start(out=wband[1][:], in_=k.dram[wname].ap()[1]
                        ).then_inc(k.sem["sg"], 16)
            for co in range(NCHUNK):
                e.wait_ge(k.sem["dv"], dv_e[co])
                e.dma_start(out=qtc[0:64, :], in_=qts[64:128, :],
                            accum_op=ALU.add).then_inc(k.sem["sg"], 16)
                e.dma_start(out=qtc[64:128, :], in_=qts[0:64, :],
                            accum_op=ALU.add).then_inc(k.sem["sg"], 16)
                e.wait_ge(k.sem["sg"], sg_sw[co])
                if which == "q":
                    dst = k.dram["q_scr"].ap()[co]
                else:
                    dst = k.dram["k_cci"].ap()[co // 4][(co % 4) * 128:
                                                        (co % 4 + 1) * 128, :]
                e.dma_start(out=dst, in_=qtc[:]).then_inc(k.sem["sg"], 16)
                if co + 2 < NCHUNK:
                    e.wait_ge(k.sem["pe"], pe_co[co])
                    e.dma_start(out=wband[(co + 2) % 2][:],
                                in_=k.dram[wname].ap()[co + 2]
                                ).then_inc(k.sem["sg"], 16)
        k.on("gpsimd", f_g)

        def f_t(e, sg_w=list(sg_w), dv_e=dv_e):
            e.wait_ge(k.sem["dv"], max(ln_dv, gate_dv))
            e.wait_ge(k.sem["ac"], gate_ac)
            for co in range(NCHUNK):
                e.wait_ge(k.sem["sg"], sg_w[co])
                if co >= 2:
                    e.wait_ge(k.sem["dv"], dv_e[co - 2])
                for s in range(4):
                    for ci in range(NCHUNK):
                        mm = e.matmul(
                            pp[4 * (co % 2) + s][:],
                            wband[co % 2][:, ci * 128:(ci + 1) * 128],
                            k.XN[:, ci * R + s * 512: ci * R + (s + 1) * 512],
                            start=(ci == 0), stop=(ci == NCHUNK - 1))
                        if ci == NCHUNK - 1:
                            mm.then_inc(k.sem["pe"], 1)
        k.on("tensor", f_t)

        def f_v(e, pe_co=pe_co, sg_st=sg_st, sg_sw=sg_sw, which=which,
                style=style, dv_e=dv_e, qs_dv=qs_dv):
            e.wait_ge(k.sem["sx"], sx_tab)
            last_dv = [qs_dv[co] if (style == "a" and which == "q") else dv_e[co]
                       for co in range(NCHUNK)]
            for co in range(NCHUNK):
                e.wait_ge(k.sem["pe"], pe_co[co])
                if co >= 1:
                    e.wait_ge(k.sem["sg"], sg_st[co - 1])
                    e.wait_ge(k.sem["dv"], last_dv[co - 1])
                for s in range(4):
                    e.tensor_mul(qtc[:, s * 512:(s + 1) * 512],
                                 pp[4 * (co % 2) + s][:],
                                 tabC[:, s * 512:(s + 1) * 512]
                                 ).then_inc(k.sem["dv"], 1)
                for s in range(4):
                    e.tensor_mul(qts[:, s * 512:(s + 1) * 512],
                                 pp[4 * (co % 2) + s][:],
                                 tabS[:, s * 512:(s + 1) * 512]
                                 ).then_inc(k.sem["dv"], 1)
                if style == "a" and which == "q":
                    e.wait_ge(k.sem["sg"], sg_sw[co])
                    e.scalar_tensor_tensor(qts[:], qtc[:], 1.0, wtb[:],
                                           ALU.mult, ALU.mult,
                                           accum_out=k.qsum[:, co:co + 1]
                                           ).then_inc(k.sem["dv"], 1)
        k.on("vector", f_v)
        if which == "q":
            k.q_store_sg = list(sg_st)
            if style == "a":
                k.qsum_dv = k.cur("dv")
        else:
            k.k_store_sg = sg_st[-1]

    k.close_scope()
    k.open_scope()
    v_gate_dv = k.cur("dv")
    vblk = [k.sbl(f"vblk{sub}_{i}", [128, 32 * 256], BF16) for i in range(2)]
    vev = [k.sbl(f"vev{sub}_{i}", [128, 256], BF16) for i in range(4)]
    vp = [k.psl(f"vp{sub}_{i}", [128, 512], F32) for i in range(4)]
    vb_sg = [k.inc("sg", 16) for _ in range(16)]
    pe_v, dv_v, sc_v = [], [], []
    for j in range(256):
        pe_v.append(k.inc("pe"))
        dv_v.append(k.inc("dv"))
        sc_v.append(k.inc("ssc", 16))

    def f_gv(e, wv=wv, vb_sg=vb_sg, pe_v=pe_v):
        for i in range(16):
            if i >= 2:
                e.wait_ge(k.sem["pe"], pe_v[(i - 2) * 16 + 15])
            e.dma_start(out=vblk[i % 2][:], in_=k.dram[wv].ap()[i]
                        ).then_inc(k.sem["sg"], 16)
    k.on("gpsimd", f_gv)

    def f_tv(e, vb_sg=vb_sg, dv_v=dv_v):
        for i in range(16):
            e.wait_ge(k.sem["sg"], vb_sg[i])
            if i == 0:
                e.wait_ge(k.sem["dv"], max(ln_dv, v_gate_dv))
            for rc in range(16):
                j = i * 16 + rc
                if j >= 4:
                    e.wait_ge(k.sem["dv"], dv_v[j - 4])
                for ci in range(NCHUNK):
                    mm = e.matmul(
                        vp[j % 4][:, 0:256],
                        k.XN[:, ci * R + rc * 128: ci * R + (rc + 1) * 128],
                        vblk[i % 2][:, ci * 256:(ci + 1) * 256],
                        start=(ci == 0), stop=(ci == NCHUNK - 1))
                    if ci == NCHUNK - 1:
                        mm.then_inc(k.sem["pe"], 1)
    k.on("tensor", f_tv)

    def f_vv(e, pe_v=pe_v, dv_v=dv_v, sc_v=sc_v):
        for j in range(256):
            e.wait_ge(k.sem["pe"], pe_v[j])
            if j >= 4:
                e.wait_ge(k.sem["ssc"], sc_v[j - 4])
            e.tensor_copy(vev[j % 4][:], vp[j % 4][:, 0:256]).then_inc(k.sem["dv"], 1)
    k.on("vector", f_vv)

    def f_sv(e, dv_v=dv_v):
        for j in range(256):
            i, rc = j // 16, j % 16
            cog, nh = i // 2, i % 2
            e.wait_ge(k.sem["dv"], dv_v[j])
            e.dma_start(out=k.dram["v_cci"].ap()[cog][rc * 128:(rc + 1) * 128,
                                                      nh * 256:(nh + 1) * 256],
                        in_=vev[j % 4][:]).then_inc(k.sem["ssc"], 16)
    k.on("scalar", f_sv)

    pre = "a" if style == "a" else "b"
    cck = [f"{pre}k{hg}" for hg in range(8)]
    ccv = [f"{pre}v{hg}" for hg in range(8)]
    for n in cck + ccv:
        k.inc(n, 1)
    k_sg = k.k_store_sg

    def f_cc(e, cck=cck, ccv=ccv, k_sg=k_sg, sc_v=sc_v):
        e.wait_ge(k.sem["sg"], k_sg)
        for hg in range(8):
            e.wait_ge(k.sem["ssc"], sc_v[(2 * hg + 1) * 16 + 15])
            e.collective_compute(
                "AllGather", ALU.bypass,
                replica_groups=[[0, 1], [2, 3], [4, 5], [6, 7]],
                ins=[k.dram["k_cci"].ap()[hg]],
                outs=[k.dram["k_cco"].ap()[hg]],
            ).then_inc(k.sem[cck[hg]])
            e.collective_compute(
                "AllGather", ALU.bypass,
                replica_groups=[[0, 1], [2, 3], [4, 5], [6, 7]],
                ins=[k.dram["v_cci"].ap()[hg]],
                outs=[k.dram["v_cco"].ap()[hg]],
            ).then_inc(k.sem[ccv[hg]])
    k.on("gpsimd", f_cc)
    k.cck, k.ccv = cck, ccv
    k.close_scope()


def emit_mask(k):
    k.open_scope()
    imp_ps = k.psl("imp_ps", [1, 512], F32)
    kvis = k.sbl("kvis", [128, 32 * 513], BF16)
    impb = k.sbl("impb", [128, 512], F32)
    impK = k.sbl("impK", [128, 5], F32)
    prow = k.sbl("prow", [1, 640], F32)
    tmpm = k.sbl("tmpm", [128, 512], F32)
    rankc = k.sbl("rankc", [128, 1], F32)
    mrow = k.sbl("mrow", [1, 512], F32)
    gate_dv, gate_ac = k.gate()

    sx_kv = [k.inc("sx", 16 * 8) for _ in range(8)]
    cck = list(k.cck)

    def f_skv(e, sx_kv=sx_kv, cck=cck):
        e.wait_ge(k.sem["dv"], gate_dv)
        for hg in range(8):
            e.wait_ge(k.sem[cck[hg]], 1)
            for hl in range(4):
                base = (hg * 4 + hl) * 513
                e.dma_start(out=kvis[:, base:base + 512],
                            in_=k.dram["k_cco"].ap()[hg][hl * 128:(hl + 1) * 128,
                                                         0:512]
                            ).then_inc(k.sem["sx"], 16)
                e.dma_start(out=kvis[:, base + 512:base + 513],
                            in_=k.dram["k_cco"].ap()[hg][512 + hl * 128:
                                                         512 + hl * 128 + 128, 0:1],
                            allow_slow_non_contiguous=True
                            ).then_inc(k.sem["sx"], 16)
    k.on("sync", f_skv)

    qsum_dv = k.qsum_dv
    dv_cast = k.inc("dv")

    def f_vc(e):
        e.wait_ge(k.sem["dv"], qsum_dv)
        e.tensor_copy(k.qsum_bf[:], k.qsum[:]).then_inc(k.sem["dv"], 1)
    k.on("vector", f_vc)

    pe_imp = k.inc("pe", 64)

    def f_t(e, sx_kv=sx_kv):
        e.wait_ge(k.sem["dv"], max(dv_cast, gate_dv))
        e.wait_ge(k.sem["ac"], gate_ac)
        first = True
        for hg in range(8):
            e.wait_ge(k.sem["sx"], sx_kv[hg])
            for hl in range(4):
                ci = 4 * hg + hl
                base = (hg * 4 + hl) * 513
                e.matmul(imp_ps[0:1, 0:511], k.qsum_bf[:, ci:ci + 1],
                         kvis[:, base + 1:base + 512],
                         start=first, stop=False).then_inc(k.sem["pe"], 1)
                e.matmul(imp_ps[0:1, 511:512], k.qsum_bf[:, ci:ci + 1],
                         kvis[:, base + 512:base + 513],
                         start=False, stop=(hg == 7 and hl == 3)
                         ).then_inc(k.sem["pe"], 1)
                first = False
    k.on("tensor", f_t)

    pe_done = k.cur("pe")
    dv_imp = k.inc("dv")
    sg_imp = k.inc("sg", 16)
    k.inc("im0", 1)
    sg_ld = k.inc("sg", 32)

    def f_v(e):
        e.wait_ge(k.sem["pe"], pe_done)
        e.tensor_copy(mrow[0:1, :], imp_ps[0:1, :]).then_inc(k.sem["dv"], 1)
    k.on("vector", f_v)

    def f_g(e):
        e.wait_ge(k.sem["dv"], dv_imp)
        e.dma_start(out=k.dram["imp_d"].ap(), in_=mrow[0:1, :]
                    ).then_inc(k.sem["sg"], 16)
        e.wait_ge(k.sem["sg"], sg_imp)
        e.collective_compute(
            "AllReduce", ALU.add,
            replica_groups=[[0, 1, 2, 3, 4, 5, 6, 7]],
            ins=[k.dram["imp_d"].ap()], outs=[k.dram["imp_ar"].ap()],
        ).then_inc(k.sem["im0"])
        e.wait_ge(k.sem["im0"], 1)
        e.dma_start(out=mrow[0:1, :], in_=k.dram["imp_ar"].ap()
                    ).then_inc(k.sem["sg"], 16)
        e.dma_start(out=impb[:],
                    in_=bass.AP(k.dram["imp_ar"], 0, [[0, 128], [1, 512]])
                    ).then_inc(k.sem["sg"], 16)
    k.on("gpsimd", f_g)

    dv_p1 = k.inc("dv")
    dv_p2 = k.inc("dv")
    sg_pad = k.inc("sg", 16)
    sg_impk = k.inc("sg", 16 * 5)

    def f_v2(e):
        e.memset(prow[0:1, :], 1.0e30).then_inc(k.sem["dv"], 1)
        e.wait_ge(k.sem["sg"], sg_ld)
        e.wait_ge(k.sem["dv"], dv_p1)
        e.tensor_copy(prow[0:1, 1:513], mrow[0:1, :]).then_inc(k.sem["dv"], 1)
    k.on("vector", f_v2)

    def f_g2(e):
        e.wait_ge(k.sem["dv"], dv_p2)
        e.dma_start(out=k.dram["pad_d"].ap(), in_=prow[0:1, :]
                    ).then_inc(k.sem["sg"], 16)
        e.wait_ge(k.sem["sg"], sg_pad)
        for kc in range(5):
            e.dma_start(out=impK[:, kc:kc + 1],
                        in_=bass.AP(k.dram["pad_d"], kc * 128, [[1, 128], [1, 1]])
                        ).then_inc(k.sem["sg"], 16)
    k.on("gpsimd", f_g2)

    dv_msk = [k.inc("dv", 3) for _ in range(5)]

    def f_v3(e, dv_msk=dv_msk):
        e.wait_ge(k.sem["sg"], sg_impk)
        for kc in range(5):
            e.tensor_scalar(tmpm[:], impb[:], impK[:, kc:kc + 1], None,
                            ALU.is_gt).then_inc(k.sem["dv"], 1)
            e.wait_ge(k.sem["dv"], dv_msk[kc] - 2)
            e.tensor_reduce(rankc[:, 0:1], tmpm[:], AXX, ALU.add
                            ).then_inc(k.sem["dv"], 1)
            e.wait_ge(k.sem["dv"], dv_msk[kc] - 1)
            e.tensor_single_scalar(k.kmask[:, kc:kc + 1], rankc[:, 0:1],
                                   256.0, ALU.is_lt).then_inc(k.sem["dv"], 1)
    k.on("vector", f_v3)
    dv_vb = k.inc("dv")

    def f_v4(e, dv_vb=dv_vb):
        e.wait_ge(k.sem["dv"], dv_vb - 1)
        e.tensor_scalar(k.vbias[:, 0:5], k.kmask[:, 0:5], 1.0, 1.0e30,
                        ALU.subtract, ALU.mult).then_inc(k.sem["dv"], 1)
    k.on("vector", f_v4)
    k.kmask_dv = k.cur("dv")
    k.close_scope()


def emit_attention(k, sub, style):
    k.open_scope()
    gate_dv, gate_ac = k.gate()
    S_ps = [k.psl(f"S_ps{sub}_{i}", [128, 512], F32) for i in range(4)]
    O_ps = k.psl(f"O_ps{sub}", [128, 512], F32)
    d_ps = k.psl(f"d_ps{sub}", [1, 512], F32)
    b_ps = k.psl(f"b_ps{sub}", [128, 512], F32)
    Kh = [k.sbl(f"Kh{sub}_{i}", [128, 4096], BF16) for i in range(2)]
    Vh = [k.sbl(f"Vh{sub}_{i}", [128, 4096], BF16) for i in range(2)]
    Qh = [k.sbl(f"Qh{sub}_{i}", [128, R], BF16) for i in range(2)]
    expS = [k.sbl(f"expS{sub}_{i}", [128, 512], BF16) for i in range(6)]
    bcs = k.sbl(f"bcs{sub}", [128, 512], F32)
    qpos = k.sbl(f"qpos{sub}", [128, R], F32)
    rrow = k.sbl(f"rrow{sub}", [1, 512], F32)
    rrow_bf = k.sbl(f"rrow_bf{sub}", [1, 512], BF16)
    cmask = k.sbl(f"cmask{sub}", [128, 8 * 512], BF16)

    is_a = style == "a"
    vis = 5 if is_a else 0
    cck, ccv, qsg = list(k.cck), list(k.ccv), list(k.q_store_sg)

    sx_qp = k.inc("sx", 16)

    def f_qp(e):
        e.wait_ge(k.sem["dv"], gate_dv)
        e.dma_start(out=qpos[:], in_=k.dram["qposb"].ap()).then_inc(k.sem["sx"], 16)
    k.on("sync", f_qp)

    dv_cm = [k.inc("dv") for _ in range(8)]

    def f_cm(e, dv_cm=dv_cm):
        e.wait_ge(k.sem["sx"], sx_qp)
        e.wait_ge(k.sem["dv"], gate_dv)
        for j in range(8):
            e.tensor_scalar(cmask[:, j * 512:(j + 1) * 512],
                            qpos[:, 0:512], k.kpos[:, j:j + 1], -1.0e30,
                            ALU.is_lt, ALU.mult).then_inc(k.sem["dv"], 1)
    k.on("vector", f_cm)
    cm_dv = dv_cm[-1]

    pe_s, ac_s, dv_s = {}, {}, {}
    for h in range(NH):
        for g in range(4):
            nk = 8 * g + 8
            qk = [k.inc("pe") for _ in range(nk)]
            ac = [k.inc("ac") for _ in range(nk)]
            pv_stop = k.inc("pe")
            den_stop = k.inc("pe")
            recip = k.inc("dv")
            rbf = k.inc("dv")
            bcast = k.inc("pe")
            bcsc = k.inc("dv")
            omul = k.inc("dv")
            pe_s[(h, g)] = dict(qk=qk, pv=pv_stop, den=den_stop, bc=bcast)
            ac_s[(h, g)] = ac
            dv_s[(h, g)] = dict(recip=recip, rbf=rbf, bcs=bcsc, omul=omul)

    def prev_hg(h, g):
        if g > 0:
            return (h, g - 1)
        if h > 0:
            return (h - 1, 3)
        return None

    sx_h = [k.inc("sx", 16 * 41) for _ in range(NH)]

    def f_sync(e, sx_h=sx_h):
        for h in range(NH):
            hg, hl = h // 4, h % 4
            e.wait_ge(k.sem[cck[hg]], 1)
            e.wait_ge(k.sem[ccv[hg]], 1)
            e.wait_ge(k.sem["sg"], qsg[h])
            if h >= 2:
                e.wait_ge(k.sem["pe"], pe_s[(h - 2, 3)]["den"])
            buf = h % 2
            for j in range(8):
                rnk, lb = j % 2, j // 2
                e.dma_start(
                    out=Kh[buf][:, j * 512:(j + 1) * 512],
                    in_=k.dram["k_cco"].ap()[hg][rnk * 512 + hl * 128:
                                                 rnk * 512 + (hl + 1) * 128,
                                                 lb * 512:(lb + 1) * 512]
                ).then_inc(k.sem["sx"], 16)
            for kc in range(32):
                j = kc // 4
                rnk, lb = j % 2, j // 2
                row = rnk * 2048 + lb * 512 + (kc % 4) * 128
                e.dma_start(
                    out=Vh[buf][:, kc * 128:(kc + 1) * 128],
                    in_=k.dram["v_cco"].ap()[hg][row:row + 128,
                                                 hl * 128:(hl + 1) * 128]
                ).then_inc(k.sem["sx"], 16)
            e.dma_start(out=Qh[buf][:], in_=k.dram["q_scr"].ap()[h]
                        ).then_inc(k.sem["sx"], 16)
    k.on("sync", f_sync)

    def f_t(e, sx_h=sx_h):
        e.wait_ge(k.sem["dv"], max(gate_dv, cm_dv))
        e.wait_ge(k.sem["ac"], gate_ac)
        for h in range(NH):
            buf = h % 2
            e.wait_ge(k.sem["sx"], sx_h[h])
            for g in range(4):
                nk = 8 * g + 8
                sc, dvs = pe_s[(h, g)], dv_s[(h, g)]
                pv = prev_hg(h, g)
                def pvden(j):
                    e.wait_ge(k.sem["ac"], ac_s[(h, g)][j])
                    if j == 0 and pv is not None:
                        e.wait_ge(k.sem["dv"], dv_s[pv]["omul"])
                    last = j == nk - 1
                    mm = e.matmul(O_ps[:], Vh[buf][:, j * 128:(j + 1) * 128],
                                  expS[j % 6][:], start=(j == 0), stop=last)
                    if last:
                        mm.then_inc(k.sem["pe"], 1)
                    mm = e.matmul(d_ps[0:1, :], k.ones_c[:], expS[j % 6][:],
                                  start=(j == 0), stop=last)
                    if last:
                        mm.then_inc(k.sem["pe"], 1)

                for kc in range(nk):
                    if kc >= 4:
                        e.wait_ge(k.sem["ac"], ac_s[(h, g)][kc - 4])
                    elif pv is not None:
                        e.wait_ge(k.sem["ac"], ac_s[pv][-1])
                    banded = kc >= 8 * g
                    mm = e.matmul(S_ps[kc % 4][:],
                                  Kh[buf][:, kc * 128:(kc + 1) * 128],
                                  Qh[buf][:, g * 512:(g + 1) * 512],
                                  start=True, stop=not banded)
                    if banded:
                        jj = kc - 8 * g
                        mm = e.matmul(S_ps[kc % 4][:], k.ident[:],
                                      cmask[:, jj * 512:(jj + 1) * 512],
                                      start=False, stop=True)
                    mm.then_inc(k.sem["pe"], 1)
                    if kc >= 2:
                        pvden(kc - 2)
                pvden(nk - 2)
                pvden(nk - 1)
                e.wait_ge(k.sem["dv"], dvs["rbf"])
                e.matmul(b_ps[:], k.ones_r[:], rrow_bf[0:1, :],
                         start=True, stop=True).then_inc(k.sem["pe"], 1)
    k.on("tensor", f_t)

    def f_a(e):
        e.wait_ge(k.sem["ac"], gate_ac)
        e.wait_ge(k.sem["dv"], gate_dv)
        for h in range(NH):
            for g in range(4):
                nk = 8 * g + 8
                for kc in range(nk):
                    e.wait_ge(k.sem["pe"], pe_s[(h, g)]["qk"][kc])
                    if is_a and kc < vis:
                        e.activation(expS[kc % 6][:], S_ps[kc % 4][:], AF.Exp,
                                     scale=SCALE, bias=k.vbias[:, kc:kc + 1]
                                     ).then_inc(k.sem["ac"], 1)
                    else:
                        e.activation(expS[kc % 6][:], S_ps[kc % 4][:], AF.Exp,
                                     scale=SCALE).then_inc(k.sem["ac"], 1)
    k.on("scalar", f_a)

    def f_v(e):
        e.wait_ge(k.sem["dv"], gate_dv)
        for h in range(NH):
            for g in range(4):
                dvs = dv_s[(h, g)]
                e.wait_ge(k.sem["pe"], pe_s[(h, g)]["den"])
                e.wait_ge(k.sem["dv"], dvs["recip"] - 1)
                e.reciprocal(rrow[0:1, :], d_ps[0:1, :]).then_inc(k.sem["dv"], 1)
                e.wait_ge(k.sem["dv"], dvs["recip"])
                e.tensor_copy(rrow_bf[0:1, :], rrow[0:1, :]).then_inc(k.sem["dv"], 1)
                e.wait_ge(k.sem["pe"], pe_s[(h, g)]["bc"])
                e.tensor_copy(bcs[:], b_ps[:]).then_inc(k.sem["dv"], 1)
                e.wait_ge(k.sem["dv"], dvs["bcs"])
                e.tensor_mul(k.XN[:, h * R + g * 512: h * R + (g + 1) * 512],
                             O_ps[:], bcs[:]).then_inc(k.sem["dv"], 1)
    k.on("vector", f_v)
    k.attn_done_dv = dv_s[(NH - 1, 3)]["omul"]
    k.close_scope()


def emit_ow(k, sub, style, dst, ready=None):
    src = "x0" if sub == 0 else "xres"
    wname = "wo" if style == "a" else "wob"
    k.open_scope()
    gate_dv, gate_ac = k.gate()
    pp = [k.psl(f"ppo{sub}_{i}", [128, 512], F32) for i in range(8)]
    wband = [k.sbl(f"wbo{sub}_{i}", [128, 4096], BF16) for i in range(2)]
    xo = [k.sbl(f"xo{sub}_{i}", [128, R], F32) for i in range(2)]
    xn = [k.sbl(f"xnw{sub}_{i}", [128, 512], F32) for i in range(4)]

    pe_co = [k.inc("pe", 4) for _ in range(NCHUNK)]
    sg_w = [k.inc("sg", 16) for _ in range(NCHUNK)]
    sx_o, dv_e, sc_st = [], [], []
    for co in range(NCHUNK):
        sx_o.append(k.inc("sx", 16))
        dv_e.append(k.inc("dv", 4))
        sc_st.append(k.inc("ssc", 16 * 4))

    def f_g(e, sg_w=sg_w, pe_co=pe_co, wname=wname):
        e.wait_ge(k.sem["dv"], gate_dv)
        for co in range(NCHUNK):
            if co >= 2:
                e.wait_ge(k.sem["pe"], pe_co[co - 2])
            e.dma_start(out=wband[co % 2][:], in_=k.dram[wname].ap()[co]
                        ).then_inc(k.sem["sg"], 16)
    k.on("gpsimd", f_g)

    def f_s(e, sx_o=sx_o, dv_e=dv_e, src=src, ready=ready):
        e.wait_ge(k.sem["dv"], gate_dv)
        for co in range(NCHUNK):
            if ready is not None:
                e.wait_ge(k.sem[ready[0]], ready[1][co])
            if co >= 2:
                e.wait_ge(k.sem["dv"], dv_e[co - 2])
            e.dma_start(out=xo[co % 2][:],
                        in_=k.dram[src].ap()[co * 128:(co + 1) * 128, :]
                        ).then_inc(k.sem["sx"], 16)
    k.on("sync", f_s)

    def f_t(e, sg_w=sg_w, dv_e=dv_e):
        e.wait_ge(k.sem["dv"], gate_dv)
        e.wait_ge(k.sem["ac"], gate_ac)
        for co in range(NCHUNK):
            e.wait_ge(k.sem["sg"], sg_w[co])
            if co >= 2:
                e.wait_ge(k.sem["dv"], dv_e[co - 2])
            for s in range(4):
                for ci in range(NCHUNK):
                    mm = e.matmul(pp[4 * (co % 2) + s][:],
                                  wband[co % 2][:, ci * 128:(ci + 1) * 128],
                                  k.XN[:, ci * R + s * 512: ci * R + (s + 1) * 512],
                                  start=(ci == 0), stop=(ci == NCHUNK - 1))
                    if ci == NCHUNK - 1:
                        mm.then_inc(k.sem["pe"], 1)
    k.on("tensor", f_t)

    def f_v(e, pe_co=pe_co, sx_o=sx_o, sc_st=sc_st):
        for co in range(NCHUNK):
            e.wait_ge(k.sem["pe"], pe_co[co])
            e.wait_ge(k.sem["sx"], sx_o[co])
            if co >= 1:
                e.wait_ge(k.sem["ssc"], sc_st[co - 1])
            for s in range(4):
                e.tensor_add(xn[s][:], pp[4 * (co % 2) + s][:],
                             xo[co % 2][:, s * 512:(s + 1) * 512]
                             ).then_inc(k.sem["dv"], 1)
    k.on("vector", f_v)

    def f_st(e, dv_e=dv_e, dst=dst):
        for co in range(NCHUNK):
            e.wait_ge(k.sem["dv"], dv_e[co])
            for s in range(4):
                e.dma_start(out=k.dram[dst].ap()[co * 128:(co + 1) * 128,
                                                 s * 512:(s + 1) * 512],
                            in_=xn[s][:]).then_inc(k.sem["ssc"], 16)
    k.on("scalar", f_st)
    k.res_sc = ("ssc", list(sc_st))
    k.close_scope()


def emit_ffn(k, sub, src, dst, ready=None):
    emit_ln(k, sub, src, ready)
    ln_dv = k.ln_done_dv
    k.open_scope()
    gate_dv, gate_ac = k.gate()
    u_ps = [k.psl(f"u_ps{sub}_{i}", [128, 512], F32) for i in range(2)]
    dn_ps = [k.psl(f"dn_ps{sub}_{i}", [128, 512], F32) for i in range(2)]
    ub = [k.sbl(f"ub{sub}_{i}", [128, 2048], BF16) for i in range(2)]
    db = [k.sbl(f"db{sub}_{i}", [128, 2048], BF16) for i in range(2)]
    Hsb = [k.sbl(f"Hsb{sub}_{i}", [128, 4096], BF16) for i in range(2)]
    xo = [k.sbl(f"xoff{sub}_{i}", [128, R], F32) for i in range(2)]
    xn = [k.sbl(f"xnf{sub}_{i}", [128, 512], F32) for i in range(2)]

    sg_w, sx_o = [], []
    ac_g, pe_u, pe_d, dv_e, sc_st = {}, {}, {}, {}, {}
    for h in range(16):
        sg_w.append(k.inc("sg", 32))
        sx_o.append(k.inc("sx", 32))
        for s in range(4):
            for mo in range(8):
                pe_u[(h, s, mo)] = k.inc("pe")
                ac_g[(h, s, mo)] = k.inc("ac")
            for dh in range(2):
                pe_d[(h, s, dh)] = k.inc("pe")
                dv_e[(h, s, dh)] = k.inc("dv")
                sc_st[(h, s, dh)] = k.inc("ssc", 16)

    def prev_u(h, s, mo):
        i = (h * 4 + s) * 8 + mo - 2
        if i < 0:
            return None
        return (i // 32, (i % 32) // 8, i % 8)

    def prev_d(h, s, dh):
        i = (h * 4 + s) * 2 + dh - 2
        if i < 0:
            return None
        return (i // 8, (i % 8) // 2, i % 2)

    def f_g(e, sg_w=sg_w, pe_d=pe_d):
        e.wait_ge(k.sem["dv"], gate_dv)
        for h in range(16):
            if h >= 2:
                e.wait_ge(k.sem["pe"], pe_d[(h - 2, 3, 1)])
            e.dma_start(out=ub[h % 2][:], in_=k.dram["upt"].ap()[h]
                        ).then_inc(k.sem["sg"], 16)
            e.dma_start(out=db[h % 2][:], in_=k.dram["dnt"].ap()[h]
                        ).then_inc(k.sem["sg"], 16)
    k.on("gpsimd", f_g)

    def f_s(e, sx_o=sx_o, dv_e=dv_e, src=src, ready=ready):
        e.wait_ge(k.sem["dv"], gate_dv)
        for h in range(16):
            if h >= 1:
                e.wait_ge(k.sem["dv"], dv_e[(h - 1, 3, 1)])
            for dh in range(2):
                ci = 2 * h + dh
                if ready is not None:
                    e.wait_ge(k.sem[ready[0]], ready[1][ci])
                e.dma_start(out=xo[dh][:],
                            in_=k.dram[src].ap()[ci * 128:(ci + 1) * 128, :]
                            ).then_inc(k.sem["sx"], 16)
    k.on("sync", f_s)

    def f_t(e, sg_w=sg_w):
        e.wait_ge(k.sem["dv"], max(ln_dv, gate_dv))
        e.wait_ge(k.sem["ac"], gate_ac)
        for h in range(16):
            e.wait_ge(k.sem["sg"], sg_w[h])
            for s in range(4):
                for mo in range(8):
                    pu = prev_u(h, s, mo)
                    if pu is not None:
                        e.wait_ge(k.sem["ac"], ac_g[pu])
                    for ci in range(2):
                        mm = e.matmul(
                            u_ps[mo % 2][:],
                            ub[h % 2][:, (mo * 2 + ci) * 128:
                                      (mo * 2 + ci + 1) * 128],
                            k.XN[:, (2 * h + ci) * R + s * 512:
                                 (2 * h + ci) * R + (s + 1) * 512],
                            start=(ci == 0), stop=(ci == 1))
                        if ci == 1:
                            mm.then_inc(k.sem["pe"], 1)
                for dh in range(2):
                    pd = prev_d(h, s, dh)
                    if pd is not None:
                        e.wait_ge(k.sem["dv"], dv_e[pd])
                    for mi in range(8):
                        e.wait_ge(k.sem["ac"], ac_g[(h, s, mi)])
                        mm = e.matmul(
                            dn_ps[dh % 2][:],
                            db[h % 2][:, (mi * 2 + dh) * 128:
                                      (mi * 2 + dh + 1) * 128],
                            Hsb[s % 2][:, mi * 512:(mi + 1) * 512],
                            start=(mi == 0), stop=(mi == 7))
                        if mi == 7:
                            mm.then_inc(k.sem["pe"], 1)
    k.on("tensor", f_t)

    def f_a(e, dst=dst):
        e.wait_ge(k.sem["ac"], gate_ac)
        for h in range(16):
            for s in range(4):
                for mo in range(8):
                    e.wait_ge(k.sem["pe"], pe_u[(h, s, mo)])
                    if mo == 0:
                        i = h * 4 + s - 2
                        if i >= 0:
                            e.wait_ge(k.sem["pe"], pe_d[(i // 4, i % 4, 1)])
                    e.activation(Hsb[s % 2][:, mo * 512:(mo + 1) * 512],
                                 u_ps[mo % 2][:], AF.Gelu).then_inc(k.sem["ac"], 1)
                for dh in range(2):
                    ci = 2 * h + dh
                    e.wait_ge(k.sem["dv"], dv_e[(h, s, dh)])
                    e.dma_start(out=k.dram[dst].ap()[ci * 128:(ci + 1) * 128,
                                                     s * 512:(s + 1) * 512],
                                in_=xn[dh][:]).then_inc(k.sem["ssc"], 16)
    k.on("scalar", f_a)

    def f_v(e, sx_o=sx_o):
        e.wait_ge(k.sem["dv"], gate_dv)
        for h in range(16):
            e.wait_ge(k.sem["sx"], sx_o[h])
            for s in range(4):
                for dh in range(2):
                    e.wait_ge(k.sem["pe"], pe_d[(h, s, dh)])
                    i = (h * 4 + s) * 2 + dh - 2
                    if i >= 0:
                        e.wait_ge(k.sem["ssc"],
                                  sc_st[(i // 8, (i % 8) // 2, i % 2)])
                    e.tensor_add(xn[dh][:], dn_ps[dh % 2][:],
                                 xo[dh][:, s * 512:(s + 1) * 512]
                                 ).then_inc(k.sem["dv"], 1)
    k.on("vector", f_v)

    k.close_scope()


def emit_tail(k):
    finals = {n: k.cur(n) for n in ("sx", "ssc", "sg")}

    def f_g(e, v=finals["sg"]):
        e.wait_ge(k.sem["sg"], v)
    k.on("gpsimd", f_g)

    def f_s(e, v=finals["ssc"]):
        e.wait_ge(k.sem["ssc"], v)
    k.on("scalar", f_s)

    def f_sy(e, v=finals["sx"]):
        e.wait_ge(k.sem["sx"], v)
    k.on("sync", f_sy)


def kernel(**inputs):
    in_maps, perm = stage_inputs(inputs)
    nc, _ = build_program()
    res = run_bass_kernel_spmd(nc, in_maps, core_ids=list(range(NCORES)))
    out = np.empty((4, 4096, 4096), np.float32)
    for c in range(NCORES):
        b = c // 2
        tidx = _rows_of_core(c)
        nat = np.empty((R, D), np.float32)
        nat[:, perm] = res.results[c]["y"].T
        out[b, tidx, :] = nat
    return out
